# version 02
# baseline (speedup 1.0000x reference)
import numpy as np
import ml_dtypes
import concourse.bass as bass
import concourse.mybir as mybir
from concourse.bass_utils import run_bass_kernel_spmd

F32 = mybir.dt.float32
BF16 = mybir.dt.bfloat16
I32 = mybir.dt.int32
AF = mybir.ActivationFunctionType
ALU = mybir.AluOpType
AX = mybir.AxisListType
BF16NP = ml_dtypes.bfloat16

NCORES = 8
B = 2
T = 2048
D = 1024
C = 128
NT = B * T // 128
NTS = T // 128
INNER = 4096
H1, DH1 = 16, 64
NEXP = 16
MAGIC = 12582912.0

DEBUG_TAPS = set()

ENGINES = ("sync", "tensor", "vector", "scalar", "gpsimd")


class Prog:

    def __init__(self, nc):
        import os
        self.maxops = int(os.environ.get("KERNEL_MAXOPS", "0")) or None
        self.nops = 0
        self.nc = nc
        self.ops = {e: [] for e in ENGINES}
        self.sems = {}
        self.led = {}
        self.wm = {e: {} for e in ENGINES}
        self.rdy = {}
        self.taps = []
        self.lastw = {}
        self.lastr = {}

    def sem(self, name):
        if name not in self.sems:
            self.sems[name] = self.nc.alloc_semaphore(name)
            self.led[name] = 0
        return self.sems[name]

    @staticmethod
    def _flat(wait):
        for tok in wait:
            if tok is None:
                continue
            if isinstance(tok, tuple) and len(tok) == 2 and isinstance(tok[0], str):
                yield tok
            else:
                yield from Prog._flat(tok)

    OWN = {"tensor": "tS", "vector": "vS", "scalar": "sS", "gpsimd": "gS"}

    def emit(self, eng, fn, wait=(), inc=None, inc_n=1, reads=(), writes=(),
             keep_own=False):
        if self.maxops is not None and self.nops >= self.maxops:
            return None
        self.nops += 1
        own = self.OWN.get(eng)
        wait = list(self._flat(wait))
        fence_own = eng == "vector"
        for n in reads:
            lw = self.lastw.get(n)
            if lw is not None and (lw[0] != eng or fence_own):
                wait.append(lw[1])
        for n in writes:
            lw = self.lastw.get(n)
            if lw is not None and (lw[0] != eng or fence_own):
                wait.append(lw[1])
            for e2, t2 in self.lastr.get(n, {}).items():
                if e2 != eng:
                    wait.append(t2)
        w = []
        for (s, v) in wait:
            if s == own and not (keep_own or fence_own):
                continue
            self.sem(s)
            if self.wm[eng].get(s, -1) < v:
                w.append((s, v))
                self.wm[eng][s] = v
        if not (isinstance(inc, str) and inc[:2] in ("qS", "qG", "qA")):
            inc = {"sync": "yS"}.get(eng, self.OWN.get(eng, "yS"))
        self.sem(inc)
        self.led[inc] += inc_n
        tok = (inc, self.led[inc])
        self.ops[eng].append((fn, w, inc, inc_n))
        for n in reads:
            self.lastr.setdefault(n, {})[eng] = tok
        for n in writes:
            self.lastw[n] = (eng, tok)
            self.lastr[n] = {}
        return tok

    def barrier(self):
        return [(s, v) for s, v in self.led.items() if not s.startswith("_")]

    def dma(self, eng, dst, src, wait=()):
        q = {"gpsimd": "qG", "sync": "qS", "scalar": "qA"}[eng]
        k = self.led.get("_n" + q, 0)
        self.led["_n" + q] = k + 1
        qs = f"{q}{k % 8}"
        return self.emit(eng, lambda e, d=dst, s=src: e.dma_start(out=d, in_=s),
                         wait=wait, inc=qs, inc_n=16,
                         reads=_nm(src), writes=_nm(dst))

    def tap(self, name, sbuf_ap, shape, dtype, wait):
        if name not in DEBUG_TAPS:
            return
        t = self.nc.dram_tensor("tap_" + name, list(shape), dtype,
                                kind="ExternalOutput").ap()
        self.dma("sync", t, sbuf_ap, wait=wait)

    def run(self, blk):
        sems = self.sems
        for ename in ENGINES:
            opslist = self.ops[ename]

            def body(eng, opslist=opslist):
                for fn, w, inc, inc_n in opslist:
                    for (s, v) in w:
                        eng.wait_ge(sems[s], v)
                    ins = fn(eng)
                    if inc is not None and ins is not None:
                        ins.then_inc(sems[inc], inc_n)
            getattr(blk, ename)(body)


def _nm(*aps):
    out = []
    for a in aps:
        if a is None or isinstance(a, (int, float)):
            continue
        t = getattr(a, "tensor", None) or a
        n = getattr(t, "name", None)
        if n is not None:
            out.append(n)
    return out


def mm(P, out, lhsT, rhs, start=True, stop=True, wait=(), inc=None):
    return P.emit("tensor",
                  lambda e, o=out, l=lhsT, r=rhs, st=start, sp=stop:
                  e.matmul(o, l, r, start=st, stop=sp), wait=wait, inc=inc,
                  reads=_nm(lhsT, rhs) + (_nm(out) if not start else []),
                  writes=_nm(out))


def act(P, eng, out, in_, func, bias=0.0, scale=1.0, accum_out=None,
        wait=(), inc=None):
    if accum_out is not None:
        return P.emit(eng, lambda e, o=out, i=in_, f=func, b=bias, s=scale,
                      a=accum_out:
                      e.activation(out=o, in_=i, func=f, bias=b, scale=s,
                                   accum_out=a), wait=wait, inc=inc,
                      reads=_nm(in_, bias, scale), writes=_nm(out, accum_out))
    return P.emit(eng, lambda e, o=out, i=in_, f=func, b=bias, s=scale:
                  e.activation(out=o, in_=i, func=f, bias=b, scale=s),
                  wait=wait, inc=inc, reads=_nm(in_, bias, scale),
                  writes=_nm(out))


def ts_(P, eng, out, in0, s1, s2=None, op0=ALU.mult, op1=None, wait=(), inc=None):
    if op1 is None:
        return P.emit(eng, lambda e, o=out, i=in0, a=s1, p0=op0:
                      e.tensor_scalar(out=o, in0=i, scalar1=a, scalar2=None,
                                      op0=p0), wait=wait, inc=inc,
                      reads=_nm(in0, s1), writes=_nm(out))
    return P.emit(eng, lambda e, o=out, i=in0, a=s1, b=s2, p0=op0, p1=op1:
                  e.tensor_scalar(out=o, in0=i, scalar1=a, scalar2=b, op0=p0,
                                  op1=p1), wait=wait, inc=inc,
                  reads=_nm(in0, s1, s2), writes=_nm(out))


def tt_(P, eng, out, in0, in1, op, wait=(), inc=None):
    return P.emit(eng, lambda e, o=out, a=in0, b=in1, p=op:
                  e.tensor_tensor(out=o, in0=a, in1=b, op=p), wait=wait,
                  inc=inc, reads=_nm(in0, in1), writes=_nm(out))


def stt_(P, eng, out, in0, scalar, in1, op0, op1, wait=(), inc=None):
    return P.emit(eng, lambda e, o=out, a=in0, s=scalar, b=in1, p0=op0, p1=op1:
                  e.scalar_tensor_tensor(out=o, in0=a, scalar=s, in1=b,
                                         op0=p0, op1=p1), wait=wait, inc=inc,
                  reads=_nm(in0, scalar, in1), writes=_nm(out))


def red_(P, eng, out, in_, op, absval=False, wait=(), inc=None):
    tok = P.emit(eng, lambda e, o=out, i=in_, p=op, av=absval:
                 e.tensor_reduce(out=o, in_=i, axis=AX.X, op=p,
                                 apply_absolute_value=av), wait=wait, inc=inc,
                 reads=_nm(in_), writes=_nm(out))
    P.emit(eng, lambda e: e.engine_nop(), wait=[tok], keep_own=True)
    return tok


def cpy_(P, eng, out, in_, wait=(), inc=None):
    if eng == "scalar":
        return P.emit(eng, lambda e, o=out, i=in_: e.copy(out=o, in_=i),
                      wait=wait, inc=inc, reads=_nm(in_), writes=_nm(out))
    return P.emit(eng, lambda e, o=out, i=in_: e.tensor_copy(out=o, in_=i),
                  wait=wait, inc=inc, reads=_nm(in_), writes=_nm(out))


def mset_(P, eng, ap, val, wait=(), inc=None):
    if eng == "scalar":
        if val != 0.0:
            raise ValueError("scalar memset only zero")
        return P.emit(eng, lambda e, a=ap: e.memzero(a), wait=wait, inc=inc,
                      writes=_nm(ap))
    return P.emit(eng, lambda e, a=ap, v=val: e.memset(a, v), wait=wait,
                  inc=inc, writes=_nm(ap))


def recip_(P, out, in_, wait=(), inc=None):
    t1 = act(P, "scalar", out, in_, AF.Ln, wait=wait)
    return act(P, "scalar", out, out, AF.Exp, scale=-1.0, wait=[t1],
               inc=inc or "sS")



class KB:
    pass


def build():
    nc = bass.Bass("TRN2", target_bir_lowering=False, detect_race_conditions=False)
    S = KB()
    S.nc = nc
    P = Prog(nc)
    S.P = P

    S.xin = nc.dram_tensor("x", [B, T, D], F32, kind="ExternalInput").ap()
    S.out_d = nc.dram_tensor("out", [B, T, D], F32, kind="ExternalOutput").ap()
    for name, shape in [
        ("W0cat", [D, 514]), ("b0cat", [1, 512]), ("bab0", [1, 2]),
        ("Wo0T", [C, D]), ("bo0", [1, D]),
        ("WgT", [D, INNER]), ("WuT", [D, INNER]), ("WdT", [INNER, D]),
        ("W1cat", [D, 3104]), ("ba1", [1, H1]), ("Wo1", [D, D]),
        ("expTW", [NEXP, D, D]), ("expB", [NEXP, D]),
    ]:
        setattr(S, name, nc.dram_tensor(name, shape, F32, kind="ExternalInput").ap())

    S.xcur = nc.dram_tensor("xcur", [B, T, D], F32, kind="Internal").ap()
    S.y0d = nc.dram_tensor("y0d", [NT * 128, INNER], F32, kind="Internal").ap()

    S.i_idb = nc.inline_tensor(np.eye(128).astype(BF16NP), name="c_idb_d").ap()
    S.i_idf = nc.inline_tensor(np.eye(128, dtype=np.float32), name="c_idf_d").ap()
    S.i_triu = nc.inline_tensor(np.triu(np.ones((128, 128), np.float32)),
                                name="c_triu_d").ap()
    S.i_striln = nc.inline_tensor(
        -np.tril(np.ones((128, 128), np.float32), -1), name="c_striln_d").ap()
    S.i_ones_f = nc.inline_tensor(np.ones((1, 128), np.float32), name="c_onesf_d").ap()
    S.i_ones_bf = nc.inline_tensor(np.ones((1, 128)).astype(BF16NP),
                                   name="c_onesbf_d").ap()
    S.i_onec_f = nc.inline_tensor(np.ones((128, 1), np.float32), name="c_onecf_d").ap()
    S.i_id64s = nc.inline_tensor(
        np.vstack([np.eye(64), np.eye(64)]).astype(BF16NP), name="c_id64s_d").ap()
    S.i_eps = nc.inline_tensor(np.full((128, 1), 1e-5, np.float32), name="c_eps_d").ap()

    A = nc.alloc_sbuf_tensor

    S.c_idb = A("c_idb", [128, 128], BF16)
    S.c_idf = A("c_idf", [128, 128], F32)
    S.c_triu = A("c_triu", [128, 128], F32)
    S.c_striln = A("c_striln", [128, 128], F32)
    S.c_ones_f = A("c_ones_f", [1, 128], F32)
    S.c_ones_bf = A("c_ones_bf", [1, 128], BF16)
    S.c_onec_f = A("c_onec_f", [128, 1], F32)
    S.c_id64s = A("c_id64s", [128, 64], BF16)
    S.c_eps = A("c_eps", [128, 1], F32)

    for n in ["mu0", "rn0", "nb0", "sx0", "cg0", "cu0", "sy0", "cd0",
              "mu1", "rn1", "nb1",
              "la0", "bt0", "cc0", "ncc0", "gam0", "bg0c", "gl0", "dc0",
              "ymx0"]:
        setattr(S, n, A(n, [128, NT], F32))
    S.dbgmx = A("dbgmx", [128, NT], F32)
    S.colsc = A("colsc", [128, 16], F32)
    S.swc = A("swc", [128, 8], F32)
    S.rowsc = A("rowsc", [1, 128], F32)
    S.brow5 = A("brow5", [1, 512], F32)
    S.browD = A("browD", [1, D], F32)
    S.brow2 = A("brow2", [1, 2], F32)
    S.browH = A("browH", [1, H1], F32)
    S.ba0bc = A("ba0bc", [128, 1], F32)
    S.bb0bc = A("bb0bc", [128, 1], F32)
    S.ba1bc = A("ba1bc", [128, H1], F32)

    S.xt = [A(f"xt{i}", [128, D], F32) for i in range(2)]
    S.xo = [A(f"xo{i}", [128, D], F32) for i in range(2)]
    S.sqs = A("sqs", [128, 1024], F32)
    S.xnb = A("xnb", [128, 1024], BF16)
    S.gw1 = A("gw1", [128, NT * NEXP], F32)
    S.s1all = A("s1all", [128, 8 * 65], BF16)
    S.d1sb = A("d1sb", [128, 128], F32)
    S.d1m = A("d1m", [128, 128], F32)
    S.pd1sb = A("pd1sb", [128, 128], BF16)
    S.glib = A("glib", [128, 64], BF16)
    S.dv1 = A("dv1", [128, 65], BF16)
    S.o65 = A("o65", [128, 65], F32)
    S.o65b = A("o65b", [128, 65], F32)
    S.o128 = A("o128", [128, 128], F32)

    S.actT = A("actT", [128, NT * D], BF16)

    AR, _ = nc.bump_sbuf(104448)
    at = lambda name, shape, dtype, off: nc.alloc_sbuf_tensor_at(
        name, shape, dtype, offset=AR + off)

    S.Ktm = at("Ktm", [128, NT * C], BF16, 0)
    S.Vtm = at("Vtm", [128, NT * C], BF16, 8192)
    S.Gtm = at("Gtm", [128, NT * C], BF16, 16384)
    S.Kfm = at("Kfm", [128, NT * C], BF16, 24576)
    S.Qfm = at("Qfm", [128, NT * C], BF16, 32768)
    S.w0mov = at("w0mov", [128, 8 * 514], BF16, 40960)
    S.wo0mov = at("wo0mov", [128, D], BF16, 49184)
    S.bias0 = at("bias0", [128, 512], F32, 51232)
    S.bo0bc = at("bo0bc", [128, D], F32, 53280)
    S.xnT = at("xnT", [128, 1024], BF16, 57376)
    S.qtt = at("qtt", [128, 128], BF16, 59424)
    off = 59680
    for n, b in [("Dsb", 512), ("Dpm", 512), ("m1sb", 512), ("tmpA", 512),
                 ("gli", 512), ("Mcsb", 512), ("Ncsb", 512), ("Npsb", 512),
                 ("Rst", 512), ("ktmp", 512), ("qtmp", 512)]:
        setattr(S, n, at(n, [128, 128], F32, off)); off += b
    for n, b in [("Nsb", 256), ("N2sb", 256), ("Tjsb", 256), ("Usb", 256),
                 ("Ufm", 256), ("Wfm", 256), ("dKsb", 256), ("Gpsb", 256),
                 ("ogsb", 256), ("ogf", 256), ("Rbf", 256), ("Sbf", 256)]:
        setattr(S, n, at(n, [128, 128], BF16, off)); off += b
    S.Ysb = at("Ysb", [128, 256], BF16, off); off += 512
    S.wsc = [at("wsc0", [128, 4096], F32, 69120),
             at("wsc1", [128, 4096], F32, 85504)]

    S.wgu = [at("wgu0", [128, 8 * 1024], BF16, 0),
             at("wgu1", [128, 8 * 1024], BF16, 16384)]
    S.wdmov = at("wdmov", [128, 32 * 1024], BF16, 0)
    S.ytl = [at("ytl0", [128, 1024], F32, 65536),
             at("ytl1", [128, 1024], F32, 69632)]
    S.ysl = [S.ytl[0], S.ytl[1]]
    S.yqT = at("yqT", [128, 32 * 128], BF16, 73728)
    S.qz = at("qz", [128, 1024], F32, 81920)
    S.qi = at("qi", [128, 1024], I32, 86016)
    S.qr = at("qr", [128, 1024], F32, 90112)
    S.qc = at("qc", [128, 1024], I32, 94208)
    S.qb = at("qb", [128, 1024], BF16, 98304)

    S.w1mov = at("w1mov", [128, 8 * 3104], BF16, 0)
    S.wo1mov = at("wo1mov", [128, 8 * 1024], BF16, 49664)
    S.q1fm = at("q1fm", [128, 1024], BF16, 66048)
    S.k1fm = at("k1fm", [128, 1024], BF16, 68096)
    S.k1tm = at("k1tm", [128, 1024], BF16, 70144)
    S.v1aug = at("v1aug", [128, 16 * 65], BF16, 72192)
    S.la1 = at("la1", [128, NT * H1], F32, 74368)
    S.cc1 = at("cc1", [128, NT * H1], F32, 76416)
    S.gam1 = at("gam1", [128, NT * H1], F32, 78464)
    S.gl1 = at("gl1", [128, NT * H1], F32, 80512)
    S.dc1 = at("dc1", [128, NT * H1], F32, 82560)
    S.e1 = at("e1scr", [128, 1024], F32, 84608)
    S.e2 = at("e2scr", [128, 1024], F32, 88704)
    S.q1tmp = at("q1tmp", [128, 1024], BF16, 92800)
    S.o1tm = at("o1tm", [128, 1024], BF16, 94848)
    S.o1fm = at("o1fm", [128, 1024], BF16, 96896)

    S.accH = at("accH", [128, 16 * 1024], F32, 0)
    S.wemov = [at("wemov0", [128, 8 * 1024], BF16, 65536),
               at("wemov1", [128, 8 * 1024], BF16, 81920)]
    S.ebrow = at("ebrow", [1, 1024], BF16, 98304)

    S.pA = nc.alloc_psum_tensor("pA", [128, 1024], F32)
    S.pB = nc.alloc_psum_tensor("pB", [128, 1024], F32)
    S.pC = nc.alloc_psum_tensor("pC", [128, 1024], F32)
    S.pD = nc.alloc_psum_tensor("pD", [128, 512], F32)
    S.pE = nc.alloc_psum_tensor("pE", [128, 512], F32)

    emit_program(P, S)

    import sys
    globals()["LAST_P"] = P
    globals()["LAST_S"] = S
    with nc.Block() as blk:
        P.run(blk)
    return nc


def xtile(S, tt, which="xin"):
    b, r0 = tt // NTS, (tt % NTS) * 128
    base = {"xin": S.xin, "xcur": S.xcur, "out": S.out_d}[which]
    return base[b, r0:r0 + 128, :]


def emit_quant(P, S, out_bf, in_ap, scale_col, width, clip1=False, wait=()):
    zz, ii, rr, cc = (S.qz[:, 0:width], S.qi[:, 0:width],
                      S.qr[:, 0:width], S.qc[:, 0:width])
    t0 = ts_(P, "vector", zz, in_ap, scale_col, MAGIC, ALU.mult, ALU.add,
             wait=[wait, P.rdy.get("_q_s")], inc="vS")
    P.rdy["_q_s"] = None
    t1 = act(P, "scalar", zz, zz, AF.Copy, bias=-MAGIC, wait=[t0], inc="sS")
    P.rdy["_q_s"] = t1
    t2 = cpy_(P, "gpsimd", ii, zz, wait=[t1], inc="gS")
    t3 = stt_(P, "vector", rr, ii, -1.0, zz, ALU.mult, ALU.add, wait=[t2], inc="vS")
    t4 = act(P, "scalar", cc, rr, AF.Copy, scale=2.0, wait=[t3], inc="sS")
    if clip1:
        t5 = tt_(P, "vector", rr, ii, cc, ALU.add, wait=[t4], inc="vS")
        return ts_(P, "vector", out_bf, rr, 1.0, -1.0, ALU.min, ALU.max,
                   wait=[t5], inc="vS")
    return tt_(P, "vector", out_bf, ii, cc, ALU.add, wait=[t4], inc="vS")


def emit_program(P, S):
    import os
    nph = int(os.environ.get("KERNEL_PHASES", "8"))
    phases = [prologue, l0_stats, l0_proj, l0_scan, l0_ffn, l1_stats,
              l1_attn, moe]
    for f in phases[:nph]:
        f(P, S)
    P.emit("sync", lambda e: e.nop(), wait=P.barrier())


def prologue(P, S):
    for dst, src in [(S.c_idb, S.i_idb), (S.c_idf, S.i_idf), (S.c_triu, S.i_triu),
                     (S.c_striln, S.i_striln), (S.c_ones_f, S.i_ones_f),
                     (S.c_ones_bf, S.i_ones_bf), (S.c_onec_f, S.i_onec_f),
                     (S.c_id64s, S.i_id64s), (S.c_eps, S.i_eps)]:
        P.dma("sync", dst[:], src)
    for kc in range(8):
        P.dma("gpsimd", S.w0mov[:, kc * 514:(kc + 1) * 514],
              S.W0cat[kc * 128:(kc + 1) * 128, :])
    tw0 = P.dma("gpsimd", S.wo0mov[:], S.Wo0T)
    tb5 = P.dma("sync", S.brow5[:], S.b0cat)
    tbD = P.dma("sync", S.browD[:], S.bo0)
    tb2 = P.dma("sync", S.brow2[:], S.bab0)
    tm = mm(P, S.pA[:, 0:512], S.c_ones_f[:], S.brow5[:],
            wait=[tb5, P.rdy.get("consts")], inc="tS")
    P.rdy["b0bc"] = cpy_(P, "vector", S.bias0[:], S.pA[:, 0:512], wait=[tm], inc="vS")
    tm1 = mm(P, S.pB[:, 0:512], S.c_ones_f[:], S.browD[:, 0:512], wait=[tbD], inc="tS")
    tm2 = mm(P, S.pB[:, 512:1024], S.c_ones_f[:], S.browD[:, 512:1024],
             wait=[tbD], inc="tS")
    P.rdy["bo0bc"] = cpy_(P, "vector", S.bo0bc[:], S.pB[:, 0:1024],
                          wait=[tm1, tm2], inc="vS")
    tm3 = mm(P, S.pD[:, 0:2], S.c_ones_f[:], S.brow2[:], wait=[tb2], inc="tS")
    cpy_(P, "vector", S.ba0bc[:], S.pD[:, 0:1], wait=[tm3], inc="vS")
    P.rdy["bab0bc"] = cpy_(P, "vector", S.bb0bc[:], S.pD[:, 1:2], wait=[tm3], inc="vS")
    tpsum_free = P.rdy["bab0bc"]

    srcs = [(S.WgT, D, INNER, 0), (S.WuT, D, INNER, 2), (S.WdT, INNER, D, 4)]
    last_use = [None, None]
    for (w, rows, ncols, j) in srcs:
        ntile = rows // 128
        acc_col = S.colsc[:, 8 + j // 2: 9 + j // 2]
        for i in range(ntile):
            buf = i % 2
            td = P.dma("sync", S.wsc[buf][:, 0:ncols],
                       w[i * 128:(i + 1) * 128, :], wait=[last_use[buf]])
            nch = ncols // 1024
            tchs = []
            for c4 in range(nch):
                tchs.append(act(P, "scalar", S.wsc[buf][:, c4 * 1024:(c4 + 1) * 1024],
                                S.wsc[buf][:, c4 * 1024:(c4 + 1) * 1024], AF.Abs,
                                accum_out=S.colsc[:, 11 + c4:12 + c4],
                                wait=[td], inc="sS"))
            tr = cpy_(P, "vector", S.colsc[:, 15:16], S.colsc[:, 11:12],
                      wait=[tchs[0]], inc="vS")
            for c4 in range(1, nch):
                tr = tt_(P, "vector", S.colsc[:, 15:16], S.colsc[:, 15:16],
                         S.colsc[:, 11 + c4:12 + c4], ALU.add,
                         wait=[tchs[c4]], inc="vS")
            last_use[buf] = tr
            if i == 0:
                tr = cpy_(P, "vector", acc_col, S.colsc[:, 15:16], wait=[tr], inc="vS")
            else:
                tr = tt_(P, "vector", acc_col, acc_col, S.colsc[:, 15:16],
                         ALU.add, wait=[tr], inc="vS")
            P.tap(f"csj{j}t{i}", S.colsc[:], (128, 16), F32, wait=[tr])
        P.tap(f"colsc_j{j}", S.colsc[:], (128, 16), F32, wait=[tr])
        tmm = mm(P, S.pD[0:1, 4 + j:5 + j], S.c_onec_f[:], acc_col,
                 wait=[tr, tpsum_free], inc="tS")
        tv = cpy_(P, "vector", S.colsc[0:1, 15:16], S.pD[0:1, 4 + j:5 + j],
                  wait=[tmm], inc="vS")
        tv = ts_(P, "vector", S.colsc[0:1, 15:16], S.colsc[0:1, 15:16],
                 1.0 / (rows * ncols), 1e-5, ALU.mult, ALU.max,
                 wait=[tv], inc="vS")
        tv2 = recip_(P, S.colsc[0:1, 14:15], S.colsc[0:1, 15:16], wait=[tv], inc="vS")
        ta = mm(P, S.pD[:, 8 + j:9 + j], S.c_ones_f[:], S.colsc[0:1, 14:15],
                wait=[tv2], inc="tS")
        tb = mm(P, S.pD[:, 10 + j:11 + j], S.c_ones_f[:], S.colsc[0:1, 15:16],
                wait=[tv], inc="tS")
        tsw = cpy_(P, "vector", S.swc[:, j:j + 1], S.pD[:, 8 + j:9 + j],
                   wait=[ta], inc="vS")
        tmw = cpy_(P, "vector", S.swc[:, j + 1:j + 2], S.pD[:, 10 + j:11 + j],
                   wait=[tb], inc="vS")
        P.rdy[f"sw{j}"] = tsw
        P.rdy[f"mw{j}"] = tmw
    P.rdy["w0mov"] = tw0
    P.rdy["wsc_done"] = last_use[0]
    P.rdy["wsc_done1"] = last_use[1]
    P.tap("swc", S.swc[:], (128, 8), F32, wait=[tsw, tmw])


def l0_stats(P, S):
    xuse = [None, None]
    for tt in range(NT):
        p = tt % 2
        td = P.dma("sync", S.xt[p][:], xtile(S, tt, "xin"), wait=[xuse[p]])
        cs = S.colsc
        t1 = red_(P, "vector", cs[:, 0:1], S.xt[p][:], ALU.add, wait=[td], inc="vS")
        tmu = ts_(P, "vector", S.mu0[:, tt:tt + 1], cs[:, 0:1], 1.0 / D,
                  op0=ALU.mult, wait=[t1], inc="vS")
        t2 = act(P, "scalar", S.sqs[:], S.xt[p][:], AF.Square,
                 accum_out=cs[:, 1:2], wait=[td], inc="sS")
        t3 = ts_(P, "vector", cs[:, 2:3], cs[:, 1:2], 1.0 / D, op0=ALU.mult,
                 wait=[t2], inc="vS")
        t4 = tt_(P, "vector", cs[:, 3:4], S.mu0[:, tt:tt + 1], S.mu0[:, tt:tt + 1],
                 ALU.mult, wait=[tmu], inc="vS")
        t5 = tt_(P, "vector", cs[:, 4:5], cs[:, 2:3], cs[:, 3:4], ALU.subtract,
                 wait=[t3, t4], inc="vS")
        t6 = act(P, "scalar", cs[:, 5:6], cs[:, 4:5], AF.Ln, bias=S.c_eps[:],
                 wait=[t5], inc="sS")
        trn = act(P, "scalar", S.rn0[:, tt:tt + 1], cs[:, 5:6], AF.Exp,
                  scale=-0.5, wait=[t6], inc="sS")
        tnb = stt_(P, "vector", S.nb0[:, tt:tt + 1], S.mu0[:, tt:tt + 1], -1.0,
                   S.rn0[:, tt:tt + 1], ALU.mult, ALU.mult, wait=[trn], inc="vS")
        txn = act(P, "scalar", S.sqs[:], S.xt[p][:], AF.Identity,
                  bias=S.nb0[:, tt:tt + 1], scale=S.rn0[:, tt:tt + 1],
                  wait=[tnb, trn], inc="sS")
        tmx = red_(P, "vector", S.dbgmx[:, tt:tt + 1], S.sqs[:], ALU.max,
                   absval=True, wait=[txn], inc="vS")
        tc1 = ts_(P, "vector", cs[:, 7:8], S.dbgmx[:, tt:tt + 1], 1e-5,
                  op0=ALU.max, wait=[tmx], inc="vS")
        tr2 = recip_(P, cs[:, 8:9], cs[:, 7:8], wait=[tc1], inc="vS")
        tsx = ts_(P, "vector", S.sx0[:, tt:tt + 1], cs[:, 8:9], 127.0,
                  op0=ALU.mult, wait=[tr2], inc="vS")
        tcg = stt_(P, "vector", S.cg0[:, tt:tt + 1], cs[:, 7:8], 1.0 / 127.0,
                   S.swc[:, 1:2], ALU.mult, ALU.mult,
                   wait=[tc1, P.rdy["mw0"]], inc="vS")
        tcu = stt_(P, "vector", S.cu0[:, tt:tt + 1], cs[:, 7:8], 1.0 / 127.0,
                   S.swc[:, 3:4], ALU.mult, ALU.mult,
                   wait=[tc1, P.rdy["mw2"]], inc="vS")
        xuse[p] = [txn, t1]
    P.tap("mu0", S.mu0[:], (128, NT), F32, wait=[tmu])
    P.tap("rn0", S.rn0[:], (128, NT), F32, wait=[trn])
    P.tap("sx0", S.sx0[:], (128, NT), F32, wait=[tsx, tcu, tcg])
    P.tap("cg0", S.cg0[:], (128, NT), F32, wait=[tcg, tcu])
    P.tap("dbgmx", S.dbgmx[:], (128, NT), F32, wait=[tsx])
    P.rdy["stats0"] = [tsx, txn]


def l0_proj(P, S):
    xuse = [P.rdy["stats0"], P.rdy["stats0"]]
    pD_free = None
    pA_free = None
    pC_free = None
    for tt in range(NT):
        p = tt % 2
        col = lambda t: t[:, tt:tt + 1]
        blk = lambda t: t[:, tt * C:(tt + 1) * C]
        td = P.dma("sync", S.xt[p][:], xtile(S, tt, "xin"), wait=xuse[p])
        t1 = act(P, "scalar", S.xnb[:], S.xt[p][:], AF.Identity,
                 bias=col(S.nb0), scale=col(S.rn0), wait=[td], inc="sS")
        tr = None
        for c8 in range(8):
            tr = mm(P, S.pA[:, c8 * 128:(c8 + 1) * 128],
                    S.xnb[:, c8 * 128:(c8 + 1) * 128], S.c_idb[:],
                    wait=[t1, pA_free], inc="tS")
        tx = cpy_(P, "scalar", S.xnT[:], S.pA[:, 0:1024], wait=[tr], inc="sS")
        pA_free = tx
        tq = None
        for kc in range(8):
            lhs = S.xnT[:, kc * 128:(kc + 1) * 128]
            tq = mm(P, S.pD[:], lhs, S.w0mov[:, kc * 514:kc * 514 + 512],
                    start=(kc == 0), stop=(kc == 7),
                    wait=[tx, pD_free, P.rdy["w0mov"]], inc="tS")
            tq = mm(P, S.pE[:, 0:2], lhs,
                    S.w0mov[:, kc * 514 + 512:kc * 514 + 514],
                    start=(kc == 0), stop=(kc == 7), inc="tS")
        ta = act(P, "scalar", S.colsc[:, 0:1], S.pE[:, 0:1], AF.Sigmoid,
                 bias=S.ba0bc[:], wait=[tq, P.rdy["bab0bc"]], inc="sS")
        ta = ts_(P, "vector", S.colsc[:, 0:1], S.colsc[:, 0:1], 1e-38,
                 op0=ALU.max, wait=[ta], inc="vS")
        tla = act(P, "scalar", S.colsc[:, 1:2], S.colsc[:, 0:1], AF.Ln,
                  wait=[ta], inc="sS")
        tla = ts_(P, "vector", col(S.la0), S.colsc[:, 1:2], -80.0, op0=ALU.max,
                  wait=[tla], inc="vS")
        tb = act(P, "scalar", col(S.bt0), S.pE[:, 1:2], AF.Sigmoid,
                 bias=S.bb0bc[:], wait=[tq], inc="sS")
        tcs = mm(P, S.pE[:, 256:257], S.c_triu[:], col(S.la0),
                 wait=[tla, tb, ta], inc="tS")
        tcl = mm(P, S.pE[0:1, 257:258], S.c_onec_f[:], col(S.la0), inc="tS")
        tc = cpy_(P, "vector", col(S.cc0), S.pE[:, 256:257], wait=[tcl], inc="vS")
        tnc = ts_(P, "vector", col(S.ncc0), col(S.cc0), -1.0, op0=ALU.mult,
                  wait=[tc], inc="vS")
        tclv = cpy_(P, "vector", S.colsc[0:1, 2:3], S.pE[0:1, 257:258],
                    wait=[tcl], inc="vS")
        tclb = mm(P, S.pE[:, 258:259], S.c_ones_f[:], S.colsc[0:1, 2:3],
                  wait=[tclv], inc="tS")
        tclc = cpy_(P, "vector", S.colsc[:, 3:4], S.pE[:, 258:259],
                    wait=[tclb], inc="vS")
        tg = act(P, "scalar", col(S.gam0), col(S.cc0), AF.Exp, wait=[tc], inc="sS")
        tgl = act(P, "scalar", col(S.gl0), S.colsc[:, 3:4], AF.Exp,
                  wait=[tclc], inc="sS")
        tdcm = tt_(P, "vector", S.colsc[:, 4:5], S.colsc[:, 3:4], col(S.cc0),
                   ALU.subtract, wait=[tclc, tc], inc="vS")
        tdc = act(P, "scalar", col(S.dc0), S.colsc[:, 4:5], AF.Exp,
                  wait=[tdcm], inc="sS")
        tbg = tt_(P, "vector", col(S.bg0c), col(S.bt0), col(S.gam0), ALU.mult,
                  wait=[tb, tg], inc="vS")
        tk1 = tt_(P, "vector", S.ktmp[:], S.pD[:, 128:256], S.bias0[:, 128:256],
                  ALU.add, wait=[tq, P.rdy["b0bc"]], inc="vS")
        tk2 = act(P, "scalar", S.sqs[:, 0:128], S.ktmp[:], AF.Square,
                  accum_out=S.colsc[:, 5:6], wait=[tk1], inc="sS")
        tk3 = ts_(P, "vector", S.colsc[:, 6:7], S.colsc[:, 5:6], 1e-24,
                  op0=ALU.max, wait=[tk2], inc="vS")
        tk4 = act(P, "scalar", S.colsc[:, 7:8], S.colsc[:, 6:7], AF.Ln,
                  wait=[tk3], inc="sS")
        tk5 = act(P, "scalar", S.colsc[:, 8:9], S.colsc[:, 7:8], AF.Exp,
                  scale=-0.5, wait=[tk4], inc="sS")
        tK = ts_(P, "vector", blk(S.Ktm), S.ktmp[:], S.colsc[:, 8:9],
                 op0=ALU.mult, wait=[tk5], inc="vS")
        tq1 = tt_(P, "vector", S.qtmp[:], S.pD[:, 0:128], S.bias0[:, 0:128],
                  ALU.add, wait=[tq], inc="vS")
        tQ = cpy_(P, "vector", S.qtt[:], S.qtmp[:], wait=[tq1, P.rdy.get("qtt_free")], inc="vS")
        tV = tt_(P, "vector", blk(S.Vtm), S.pD[:, 256:384], S.bias0[:, 256:384],
                 ALU.add, wait=[tq], inc="vS")
        tg1 = tt_(P, "vector", S.sqs[:, 128:256], S.pD[:, 384:512],
                  S.bias0[:, 384:512], ALU.add, wait=[tq], inc="vS")
        tG = act(P, "scalar", blk(S.Gtm), S.sqs[:, 128:256], AF.Sigmoid,
                 wait=[tg1], inc="sS")
        pD_free = [tV, tg1, tq1, tk1, ta, tb, tc, tclv, tclc]
        tt1 = mm(P, S.pC[:, 0:128], blk(S.Ktm), S.c_idb[:],
                 wait=[tK, pC_free], inc="tS")
        tt2 = mm(P, S.pC[:, 128:256], S.qtt[:], S.c_idb[:], wait=[tQ], inc="tS")
        P.rdy["qtt_free"] = tt2
        te1 = cpy_(P, "scalar", blk(S.Kfm), S.pC[:, 0:128], wait=[tt1], inc="sS")
        te2 = cpy_(P, "scalar", blk(S.Qfm), S.pC[:, 128:256], wait=[tt2], inc="sS")
        pC_free = [te1, te2]
        xuse[p] = [t1]
    P.rdy["l0proj"] = [tK, tV, tG, te1, te2, tbg, tdc, tgl, tnc, tx]
    P.tap("Ktm", S.Ktm[:, 0:C], (128, C), BF16, wait=[tK])
    P.tap("la0", S.la0[:], (128, NT), F32, wait=[tla])
    P.tap("cc0", S.cc0[:], (128, NT), F32, wait=[tc])
    P.tap("gam0", S.gam0[:], (128, NT), F32, wait=[tg])


def l0_scan(P, S):
    import os
    _lim = int(os.environ.get("KERNEL_SCAN_LIM", "99"))
    _parts = os.environ.get("KERNEL_SCAN_PARTS", "spoc")
    pd_free = P.rdy["l0proj"]
    pe_free = P.rdy["l0proj"]
    pa_free = P.rdy["l0proj"]
    pb_free = P.rdy["l0proj"]
    pc_free = P.rdy["l0proj"]
    pc5_free = P.rdy["l0proj"]
    xuse = [P.rdy["l0proj"], P.rdy["l0proj"]]
    xo_use = [None, None]
    st_rdy = None
    for seq in range(B):
        mz1 = mset_(P, "vector", S.Rst[:], 0.0, wait=[st_rdy], inc="vS")
        mz2 = mset_(P, "scalar", S.Rbf[:], 0.0, wait=[st_rdy], inc="sS")
        mz3 = mset_(P, "scalar", S.Sbf[:], 0.0, inc="sS")
        st_rdy = [mz1, mz2, mz3]
        for k in range(NTS):
            tt = seq * NTS + k
            if tt >= _lim:
                break
            col = lambda t: t[:, tt:tt + 1]
            blk = lambda t: t[:, tt * C:(tt + 1) * C]
            tkk = mm(P, S.pD[:, 0:128], blk(S.Kfm), blk(S.Kfm),
                     wait=[pd_free], inc="tS")
            trow = mm(P, S.pE[0:1, 128:256], col(S.cc0), S.c_idf[:],
                      wait=[pe_free], inc="tS")
            trv = cpy_(P, "vector", S.rowsc[:], S.pE[0:1, 128:256],
                       wait=[trow], inc="vS")
            tbc = mm(P, S.pE[:, 0:128], S.c_ones_f[:], S.rowsc[:],
                     wait=[trv], inc="tS")
            tu1 = act(P, "scalar", S.tmpA[:], S.pE[:, 0:128], AF.Relu,
                      bias=col(S.ncc0), scale=1.0, wait=[tbc], inc="sS")
            tD = act(P, "scalar", S.Dsb[:], S.tmpA[:], AF.Exp, scale=-1.0,
                     wait=[tu1], inc="sS")
            tu2 = act(P, "scalar", S.tmpA[:], S.pE[:, 0:128], AF.Relu,
                      bias=col(S.cc0), scale=-1.0, wait=[tD], inc="sS")
            tDp = act(P, "scalar", S.Dpm[:], S.tmpA[:], AF.Exp, scale=-1.0,
                      wait=[tu2], inc="sS")
            tDp = tt_(P, "vector", S.Dpm[:], S.Dpm[:], S.c_triu[:], ALU.mult,
                      wait=[tDp], inc="vS")
            pe_free = [tu2]
            tm1 = tt_(P, "vector", S.m1sb[:], S.Dsb[:], S.pD[:, 0:128],
                      ALU.mult, wait=[tD, tkk], inc="vS")
            tN = stt_(P, "vector", S.Nsb[:], S.m1sb[:], col(S.bt0), S.c_striln[:],
                      ALU.mult, ALU.mult, wait=[tm1], inc="vS")
            ty1 = ts_(P, "vector", S.Ysb[:, 0:128], blk(S.Vtm), col(S.bt0),
                      op0=ALU.mult, inc="vS")
            ty2 = ts_(P, "vector", S.Ysb[:, 128:256], blk(S.Ktm), col(S.bg0c),
                      op0=ALU.mult, inc="vS")
            tdk = ts_(P, "vector", S.dKsb[:], blk(S.Ktm), col(S.dc0),
                      op0=ALU.mult, inc="vS")
            Ncur, Nnext = S.Nsb, S.N2sb
            tNc = tN
            tY = [ty1, ty2]
            pdsolve_free = tm1
            for lvl in range(7 if "s" in _parts else 0):
                ttr = mm(P, S.pD[:, 128:256], Ncur[:], S.c_idb[:],
                         wait=[tNc, pdsolve_free], inc="tS")
                tTj = cpy_(P, "scalar", S.Tjsb[:], S.pD[:, 128:256],
                           wait=[ttr], inc="sS")
                ta1 = mm(P, S.pA[:, 0:256], S.Tjsb[:], S.Ysb[:], start=True,
                         stop=False, wait=[tTj, tY, pa_free], inc="tS")
                ta2 = mm(P, S.pA[:, 0:256], S.c_idb[:], S.Ysb[:], start=False,
                         stop=True, inc="tS")
                if lvl < 6:
                    tsq = mm(P, S.pD[:, 256:384], S.Tjsb[:], Ncur[:], inc="tS")
                    tNc = cpy_(P, "scalar", Nnext[:], S.pD[:, 256:384],
                               wait=[tsq], inc="sS")
                    pdsolve_free = tNc
                tY = [cpy_(P, "vector", S.Ysb[:], S.pA[:, 0:256], wait=[ta2],
                           inc="vS")]
                pa_free = tY
                Ncur, Nnext = Nnext, Ncur
            if "p" not in _parts:
                pd_free = [tY, tNc]
                pe_free = [tu2]
                continue
            tmw = mm(P, S.pD[:, 0:128], S.Ysb[:, 128:256], S.dKsb[:],
                     wait=[tY, tdk, tNc], inc="tS")
            tgli = ts_(P, "vector", S.gli[:], S.c_idf[:], col(S.gl0),
                       op0=ALU.mult, inc="vS")
            tMce = cpy_(P, "vector", S.Mcsb[:], S.pD[:, 0:128],
                        wait=[tmw], inc="vS")
            tMc = stt_(P, "vector", S.Mcsb[:], S.Mcsb[:], -1.0, S.gli[:],
                       ALU.mult, ALU.add, wait=[tMce, tgli], inc="vS")
            tn1 = mm(P, S.pC[:, 0:128], S.dKsb[:], S.Ysb[:, 0:128],
                     wait=[pc_free], inc="tS")
            tn2 = mm(P, S.pC[:, 128:256], S.Ysb[:, 0:128], S.dKsb[:], inc="tS")
            tNc1 = cpy_(P, "vector", S.Ncsb[:], S.pC[:, 0:128], wait=[tn2], inc="vS")
            tNc2 = cpy_(P, "scalar", S.Npsb[:], S.pC[:, 128:256], wait=[tn2], inc="sS")
            pc_free = [tNc1, tNc2]
            if "o" not in _parts:
                pd_free = [tMc]
                continue
            twf = mm(P, S.pD[:, 128:256], S.Ysb[:, 128:256], S.c_idb[:],
                     wait=[tMc], inc="tS")
            twfe = cpy_(P, "scalar", S.Wfm[:], S.pD[:, 128:256], wait=[twf], inc="sS")
            twr = mm(P, S.pA[:, 0:128], S.Wfm[:], S.Rbf[:],
                     wait=[twfe, st_rdy, pa_free], inc="tS")
            tUe = cpy_(P, "vector", S.o128[:], S.pA[:, 0:128], wait=[twr], inc="vS")
            tU = stt_(P, "vector", S.Usb[:], S.o128[:], -1.0,
                      S.Ysb[:, 0:128], ALU.mult, ALU.add, wait=[tUe], inc="vS")
            tuf = mm(P, S.pD[:, 128:256], S.Usb[:], S.c_idb[:],
                     wait=[tU, twfe], inc="tS")
            tufe = cpy_(P, "scalar", S.Ufm[:], S.pD[:, 128:256], wait=[tuf], inc="sS")
            tp = mm(P, S.pA[:, 0:128], S.Ufm[:], blk(S.Qfm),
                    wait=[tufe, tU], inc="tS")
            tGp = tt_(P, "vector", S.Gpsb[:], S.pA[:, 0:128], S.Dpm[:],
                      ALU.mult, wait=[tp, tDp], inc="vS")
            pa_free = tGp
            to1 = mm(P, S.pB[:, 0:128], S.Gpsb[:], blk(S.Ktm), start=True,
                     stop=True, wait=[tGp, pb_free], inc="tS")
            to2 = mm(P, S.pB[:, 128:256], blk(S.Qfm), S.Sbf[:], start=True,
                     stop=True, wait=[st_rdy], inc="tS")
            tos = cpy_(P, "vector", S.o128[:], S.pB[:, 128:256], wait=[to2], inc="vS")
            tom = ts_(P, "vector", S.tmpA[:], S.o128[:], col(S.gam0),
                      op0=ALU.mult, wait=[tos], inc="vS")
            tom = tt_(P, "vector", S.o128[:], S.tmpA[:], S.pB[:, 0:128],
                      ALU.add, wait=[tom, to1], inc="vS")
            tog = tt_(P, "vector", S.ogsb[:], S.o128[:], blk(S.Gtm),
                      ALU.mult, wait=[tom], inc="vS")
            tof = mm(P, S.pD[:, 128:256], S.ogsb[:], S.c_idb[:],
                     wait=[tog, tufe], inc="tS")
            tofe = cpy_(P, "scalar", S.ogf[:], S.pD[:, 128:256], wait=[tof], inc="sS")
            pd_free = tofe
            tw1 = mm(P, S.pB[:, 0:512], S.ogf[:], S.wo0mov[:, 0:512],
                     wait=[tofe, tog], inc="tS")
            tw2 = mm(P, S.pB[:, 512:1024], S.ogf[:], S.wo0mov[:, 512:1024],
                     inc="tS")
            p = tt % 2
            td = P.dma("sync", S.xt[p][:], xtile(S, tt, "xin"), wait=[xuse[p]])
            tr1 = tt_(P, "vector", S.xo[p][:], S.pB[:, 0:1024], S.bo0bc[:],
                      ALU.add, wait=[tw1, tw2, P.rdy["bo0bc"], xo_use[p]],
                      inc="vS")
            tr2 = tt_(P, "vector", S.xo[p][:], S.xo[p][:], S.xt[p][:], ALU.add,
                      wait=[td, tr1], inc="vS")
            tst = P.dma("sync", xtile(S, tt, "xcur"), S.xo[p][:], wait=[tr2])
            xuse[p] = [tr2]
            xo_use[p] = [tst]
            pb_free = tr1
            if "c" not in _parts:
                continue
            tch1 = mm(P, S.pC[:, 512:640], S.Mcsb[:], S.Rst[:], start=True,
                      stop=False, wait=[tMc, pc5_free, st_rdy], inc="tS")
            tch2 = mm(P, S.pC[:, 512:640], S.c_idf[:], S.Ncsb[:], start=False,
                      stop=True, wait=[tNc1], inc="tS")
            tch3 = mm(P, S.pC[:, 640:768], S.Rst[:], S.Mcsb[:], start=True,
                      stop=False, inc="tS")
            tch4 = mm(P, S.pC[:, 640:768], S.c_idf[:], S.Npsb[:], start=False,
                      stop=True, wait=[tNc2], inc="tS")
            tR = cpy_(P, "vector", S.Rst[:], S.pC[:, 512:640], wait=[tch4], inc="vS")
            tRb = cpy_(P, "scalar", S.Rbf[:], S.pC[:, 512:640], wait=[tch4], inc="sS")
            tSb = cpy_(P, "scalar", S.Sbf[:], S.pC[:, 640:768], wait=[tch4], inc="sS")
            st_rdy = [tR, tRb, tSb]
            pc5_free = st_rdy
    P.tap("xattn", S.xo[(NT - 1) % 2][:], (128, D), F32, wait=[tr2])
    P.tap("Ysb", S.Ysb[:], (128, 256), BF16, wait=[tY])
    P.tap("Mcsb", S.Mcsb[:], (128, 128), F32, wait=[tMc])
    P.tap("Rst", S.Rst[:], (128, 128), F32, wait=[st_rdy])
    if "xcurA" in DEBUG_TAPS:
        txA = P.nc.dram_tensor("tap_xcurA", [B, T, D], F32, kind="ExternalOutput").ap()
        P.dma("sync", txA, S.xcur, wait=[[tst], tr2])
    P.rdy["l0attn"] = [st_rdy] + ([[tst], tr2, tofe, tNc1, tNc2]
                                   if "c" in _parts and _lim >= 99 else
                                   [P.rdy["l0proj"]])
    P.pf = {"pA": pa_free, "pB": pb_free, "pC": [pc_free, pc5_free],
            "pD": pd_free, "pE": pe_free}


def l0_ffn(P, S):
    bar = P.barrier()
    xuse = [P.rdy["l0attn"], P.rdy["l0attn"]]
    qb_use = None
    pa_free = P.pf["pA"]
    for tt in range(NT):
        p = tt % 2
        td = P.dma("sync", S.xt[p][:], xtile(S, tt, "xin"), wait=[xuse[p]])
        txn = act(P, "scalar", S.sqs[:], S.xt[p][:], AF.Identity,
                  bias=S.nb0[:, tt:tt + 1], scale=S.rn0[:, tt:tt + 1],
                  wait=[td], inc="sS")
        xuse[p] = [txn]
        tqn = emit_quant(P, S, S.qb[:], S.sqs[:], S.sx0[:, tt:tt + 1], 1024,
                         wait=[txn, qb_use, bar if tt == 0 else None])
        ttr = None
        for c8 in range(8):
            ttr = mm(P, S.pA[:, c8 * 128:(c8 + 1) * 128],
                     S.qb[:, c8 * 128:(c8 + 1) * 128], S.c_idb[:],
                     wait=[tqn, pa_free], inc="tS")
        qb_use = ttr
        tev = cpy_(P, "scalar", S.actT[:, tt * D:(tt + 1) * D], S.pA[:, 0:1024],
                   wait=[ttr], inc="sS")
        pa_free = tev
    actT_rdy = tev
    P.tap("xqT", S.actT[:, 0:D], (128, D), BF16, wait=[actT_rdy])

    wscq_use = [bar, bar]
    wgu_use = [None, None]
    wgu_rdy = [None, None]
    ysl_use = [None, None]
    pD_free = P.pf["pD"]
    pE_free = P.pf["pE"]
    tmax = None
    for og in range(8):
        buf = og % 2
        tqn = None
        for m, wsrc, swcol in [(0, S.WgT, S.swc[:, 0:1]), (1, S.WuT, S.swc[:, 2:3])]:
            for kc in range(8):
                wb = (og * 16 + m * 8 + kc) % 2
                td = P.dma("sync", S.ytl[wb][:, 0:512],
                           wsrc[kc * 128:(kc + 1) * 128, og * 512:(og + 1) * 512],
                           wait=[wscq_use[wb]])
                tqn = emit_quant(
                    P, S, S.wgu[buf][:, kc * 1024 + m * 512:kc * 1024 + (m + 1) * 512],
                    S.ytl[wb][:, 0:512], swcol, 512, clip1=True,
                    wait=[td, wgu_use[buf] if wgu_use[buf] is not None else bar,
                          P.rdy["sw0"] if m == 0 else P.rdy["sw2"]])
                wscq_use[wb] = tqn
        wgu_rdy[buf] = tqn
        for tt in range(NT):
            tmm2 = None
            for kc in range(8):
                lhs = S.actT[:, tt * D + kc * 128:tt * D + (kc + 1) * 128]
                mm(P, S.pD[:], lhs, S.wgu[buf][:, kc * 1024:kc * 1024 + 512],
                   start=(kc == 0), stop=(kc == 7),
                   wait=[wgu_rdy[buf], actT_rdy, pD_free], inc="tS")
                tmm2 = mm(P, S.pE[:], lhs,
                          S.wgu[buf][:, kc * 1024 + 512:(kc + 1) * 1024],
                          start=(kc == 0), stop=(kc == 7), wait=[pE_free],
                          inc="tS")
            yb = tt % 2
            tg = act(P, "scalar", S.sqs[:, 0:512], S.pD[:], AF.Silu,
                     scale=S.cg0[:, tt:tt + 1], wait=[tmm2], inc="sS")
            tue = act(P, "scalar", S.sqs[:, 512:1024], S.pE[:], AF.Identity,
                      scale=S.cu0[:, tt:tt + 1], wait=[tmm2], inc="sS")
            tyv = tt_(P, "vector", S.ysl[yb][:, 0:512], S.sqs[:, 512:1024],
                      S.sqs[:, 0:512], ALU.mult,
                      wait=[tg, tue, ysl_use[yb]], inc="vS")
            pD_free = tg
            pE_free = tue
            tmx = red_(P, "vector", S.colsc[:, 0:1], S.ysl[yb][:, 0:512],
                       ALU.max, absval=True, wait=[tyv], inc="vS")
            if og == 0:
                tmax = cpy_(P, "vector", S.ymx0[:, tt:tt + 1], S.colsc[:, 0:1],
                            wait=[tmx], inc="vS")
            else:
                tmax = tt_(P, "vector", S.ymx0[:, tt:tt + 1], S.ymx0[:, tt:tt + 1],
                           S.colsc[:, 0:1], ALU.max, wait=[tmx], inc="vS")
            tst = P.dma("sync", S.y0d[tt * 128:(tt + 1) * 128,
                                      og * 512:(og + 1) * 512],
                        S.ysl[yb][:, 0:512], wait=[tyv])
            ysl_use[yb] = tst
        wgu_use[buf] = tmm2
    wgu_done = tmm2
    tcd = None
    for tt in range(NT):
        c1 = ts_(P, "vector", S.colsc[:, 1:2], S.ymx0[:, tt:tt + 1], 1e-5,
                 op0=ALU.max, wait=[tmax], inc="vS")
        r1 = recip_(P, S.colsc[:, 2:3], S.colsc[:, 1:2], wait=[c1], inc="vS")
        ts_(P, "vector", S.sy0[:, tt:tt + 1], S.colsc[:, 2:3], 127.0,
            op0=ALU.mult, wait=[r1], inc="vS")
        tcd = stt_(P, "vector", S.cd0[:, tt:tt + 1], S.colsc[:, 1:2],
                   1.0 / 127.0, S.swc[:, 5:6], ALU.mult, ALU.mult,
                   wait=[c1, P.rdy["mw4"]], inc="vS")
    tqn = None
    for ic in range(32):
        wb = ic % 2
        td = P.dma("sync", S.ytl[wb][:],
                   S.WdT[ic * 128:(ic + 1) * 128, :],
                   wait=[wscq_use[wb], ysl_use[wb]])
        tqn = emit_quant(P, S, S.wdmov[:, ic * 1024:(ic + 1) * 1024],
                         S.ytl[wb][:], S.swc[:, 4:5], 1024, clip1=True,
                         wait=[td, P.rdy["sw4"],
                               wgu_done if ic < 2 else None])
        wscq_use[wb] = tqn
    wd_rdy = tqn
    ytl_use = [None, None]
    xuse = [xuse[0], xuse[1]]
    xo_use = [P.rdy["l0attn"], P.rdy["l0attn"]]
    pB_free = P.pf["pB"]
    for tt in range(NT):
        tev = None
        for icg in range(4):
            yb = icg % 2
            td = P.dma("sync", S.ytl[yb][:],
                       S.y0d[tt * 128:(tt + 1) * 128, icg * 1024:(icg + 1) * 1024],
                       wait=[ytl_use[yb], wscq_use[yb]])
            tqn = emit_quant(P, S, S.qb[:], S.ytl[yb][:], S.sy0[:, tt:tt + 1],
                             1024, wait=[td, qb_use, tcd])
            ytl_use[yb] = tqn
            ttr = None
            for c8 in range(8):
                ttr = mm(P, S.pA[:, c8 * 128:(c8 + 1) * 128],
                         S.qb[:, c8 * 128:(c8 + 1) * 128], S.c_idb[:],
                         wait=[tqn, pa_free], inc="tS")
            qb_use = ttr
            tev = cpy_(P, "scalar", S.yqT[:, icg * 1024:(icg + 1) * 1024],
                       S.pA[:, 0:1024], wait=[ttr], inc="sS")
            pa_free = tev
        tdn = None
        for ic in range(32):
            lhs = S.yqT[:, ic * 128:(ic + 1) * 128]
            mm(P, S.pB[:, 0:512], lhs, S.wdmov[:, ic * 1024:ic * 1024 + 512],
               start=(ic == 0), stop=(ic == 31),
               wait=[tev, wd_rdy, pB_free], inc="tS")
            tdn = mm(P, S.pB[:, 512:1024], lhs,
                     S.wdmov[:, ic * 1024 + 512:(ic + 1) * 1024],
                     start=(ic == 0), stop=(ic == 31), inc="tS")
        p = tt % 2
        td2 = P.dma("sync", S.xt[p][:], xtile(S, tt, "xcur"), wait=[xuse[p]])
        tde = act(P, "scalar", S.sqs[:], S.pB[:, 0:1024], AF.Identity,
                  scale=S.cd0[:, tt:tt + 1], wait=[tdn, tcd], inc="sS")
        txo = tt_(P, "vector", S.xo[p][:], S.sqs[:], S.xt[p][:], ALU.add,
                  wait=[tde, td2, xo_use[p]], inc="vS")
        pB_free = tde
        tst = P.dma("sync", xtile(S, tt, "xcur"), S.xo[p][:], wait=[txo])
        xuse[p] = [txo]
        xo_use[p] = [tst]
    P.tap("xl0", S.xo[(NT - 1) % 2][:], (128, D), F32, wait=[txo])
    if "y0" in DEBUG_TAPS:
        ty = P.nc.dram_tensor("tap_y0", [128, INNER], F32, kind="ExternalOutput").ap()
        P.dma("sync", ty, S.y0d[0:128, :], wait=[ysl_use[0], ysl_use[1]])
    if "xcur0" in DEBUG_TAPS:
        tx = P.nc.dram_tensor("tap_xcur0", [B, T, D], F32, kind="ExternalOutput").ap()
        P.dma("sync", tx, S.xcur, wait=[[tst], txo])
    P.tap("yq", S.qb[:], (128, 1024), BF16, wait=[ytl_use[0], ytl_use[1]])
    P.rdy["l0done"] = [[tst], txo, tev]
    P.pf["pA"] = pa_free
    P.pf["pB"] = pB_free
    P.pf["pD"] = pD_free
    P.pf["pE"] = pE_free
    P.rdy["wdmov_used"] = tdn

def l1_stats(P, S):
    bar = P.barrier()
    tW1 = None
    for kc in range(8):
        tW1 = P.dma("gpsimd", S.w1mov[:, kc * 3104:(kc + 1) * 3104],
                    S.W1cat[kc * 128:(kc + 1) * 128, :], wait=[bar])
    P.rdy["w1mov"] = tW1
    tWo1 = None
    for kc in range(8):
        tWo1 = P.dma("gpsimd", S.wo1mov[:, kc * 1024:(kc + 1) * 1024],
                     S.Wo1[kc * 128:(kc + 1) * 128, :], wait=[bar])
    P.rdy["wo1mov"] = tWo1
    tb = P.dma("sync", S.browH[:], S.ba1, wait=[bar])
    tmb = mm(P, S.pD[:, 0:16], S.c_ones_f[:], S.browH[:],
             wait=[tb, P.pf["pD"]], inc="tS")
    tba = ts_(P, "vector", S.ba1bc[:], S.pD[:, 0:16], 2.0, op0=ALU.add,
              wait=[tmb], inc="vS")
    P.rdy["ba1bc"] = tba
    P.pf["pD"] = tba
    xuse = [P.rdy["l0done"], P.rdy["l0done"]]
    for tt in range(NT):
        p = tt % 2
        td = P.dma("sync", S.xt[p][:], xtile(S, tt, "xcur"), wait=[xuse[p]])
        cs = S.colsc
        t1 = red_(P, "vector", cs[:, 0:1], S.xt[p][:], ALU.add, wait=[td], inc="vS")
        tmu = ts_(P, "vector", S.mu1[:, tt:tt + 1], cs[:, 0:1], 1.0 / D,
                  op0=ALU.mult, wait=[t1], inc="vS")
        t2 = act(P, "scalar", S.sqs[:], S.xt[p][:], AF.Square,
                 accum_out=cs[:, 1:2], wait=[td], inc="sS")
        t3 = ts_(P, "vector", cs[:, 2:3], cs[:, 1:2], 1.0 / D, op0=ALU.mult,
                 wait=[t2], inc="vS")
        t4 = tt_(P, "vector", cs[:, 3:4], S.mu1[:, tt:tt + 1], S.mu1[:, tt:tt + 1],
                 ALU.mult, wait=[tmu], inc="vS")
        t5 = tt_(P, "vector", cs[:, 4:5], cs[:, 2:3], cs[:, 3:4], ALU.subtract,
                 wait=[t3, t4], inc="vS")
        t6 = act(P, "scalar", cs[:, 5:6], cs[:, 4:5], AF.Ln, bias=S.c_eps[:],
                 wait=[t5], inc="sS")
        trn = act(P, "scalar", S.rn1[:, tt:tt + 1], cs[:, 5:6], AF.Exp,
                  scale=-0.5, wait=[t6], inc="sS")
        tnb = stt_(P, "vector", S.nb1[:, tt:tt + 1], S.mu1[:, tt:tt + 1], -1.0,
                   S.rn1[:, tt:tt + 1], ALU.mult, ALU.mult, wait=[trn], inc="vS")
        xuse[p] = [t2, t1]
    P.rdy["stats1"] = [tnb, trn]


def l1_attn(P, S):
    pA_free = P.pf["pA"]
    pB_free = P.pf["pB"]
    pC_free = P.pf["pC"]
    pD_free = P.pf["pD"]
    pE_free = P.pf["pE"]
    xuse = [P.rdy["stats1"], P.rdy["stats1"]]
    xo_use = [P.rdy["l0done"], P.rdy["l0done"]]
    actT_free = P.rdy["l0done"]
    st_rdy = None
    buf_free = [P.rdy["stats1"]]
    for tt in range(NT):
        p = tt % 2
        if tt % NTS == 0:
            st_rdy = mset_(P, "vector", S.s1all[:], 0.0, wait=[st_rdy], inc="vS")
        td = P.dma("sync", S.xt[p][:], xtile(S, tt, "xcur"), wait=[xuse[p]])
        t1 = act(P, "scalar", S.xnb[:], S.xt[p][:], AF.Identity,
                 bias=S.nb1[:, tt:tt + 1], scale=S.rn1[:, tt:tt + 1],
                 wait=[td], inc="sS")
        ttr = None
        for c8 in range(8):
            ttr = mm(P, S.pA[:, c8 * 128:(c8 + 1) * 128],
                     S.xnb[:, c8 * 128:(c8 + 1) * 128], S.c_idb[:],
                     wait=[t1, pA_free, actT_free], inc="tS")
        txT = cpy_(P, "scalar", S.actT[:, tt * D:(tt + 1) * D],
                   S.pA[:, 0:1024], wait=[ttr], inc="sS")
        pA_free = txT
        tpj = None
        for kc in range(8):
            lhs = S.actT[:, tt * D + kc * 128:tt * D + (kc + 1) * 128]
            base = kc * 3104
            mm(P, S.pA[:, 0:512], lhs, S.w1mov[:, base:base + 512],
               start=(kc == 0), stop=(kc == 7),
               wait=[txT, P.rdy["w1mov"], pA_free], inc="tS")
            mm(P, S.pA[:, 512:1024], lhs, S.w1mov[:, base + 512:base + 1024],
               start=(kc == 0), stop=(kc == 7), inc="tS")
            mm(P, S.pB[:, 0:512], lhs, S.w1mov[:, base + 1024:base + 1536],
               start=(kc == 0), stop=(kc == 7), wait=[pB_free], inc="tS")
            mm(P, S.pB[:, 512:1024], lhs, S.w1mov[:, base + 1536:base + 2048],
               start=(kc == 0), stop=(kc == 7), inc="tS")
            mm(P, S.pC[:, 0:512], lhs, S.w1mov[:, base + 2048:base + 2560],
               start=(kc == 0), stop=(kc == 7), wait=[pC_free], inc="tS")
            mm(P, S.pC[:, 512:1024], lhs, S.w1mov[:, base + 2560:base + 3072],
               start=(kc == 0), stop=(kc == 7), inc="tS")
            tpj = mm(P, S.pD[:, 0:32], lhs, S.w1mov[:, base + 3072:base + 3104],
                     start=(kc == 0), stop=(kc == 7), wait=[pD_free], inc="tS")
        hsl = slice(tt * H1, (tt + 1) * H1)
        tab = tt_(P, "vector", S.e2[:, 32:48], S.pD[:, 0:16], S.ba1bc[:],
                  ALU.add, wait=[tpj, P.rdy["ba1bc"], buf_free], inc="vS")
        tsg = act(P, "scalar", S.e1[:, 0:16], S.e2[:, 32:48], AF.Sigmoid,
                  wait=[tab, buf_free], inc="sS")
        tsg = ts_(P, "vector", S.e1[:, 0:16], S.e1[:, 0:16], 1e-38,
                  op0=ALU.max, wait=[tsg], inc="vS")
        tla = act(P, "scalar", S.e1[:, 16:32], S.e1[:, 0:16], AF.Ln,
                  wait=[tsg], inc="sS")
        tla = ts_(P, "vector", S.la1[:, hsl], S.e1[:, 16:32], -80.0,
                  op0=ALU.max, wait=[tla, buf_free], inc="vS")
        tlg = cpy_(P, "vector", S.e2[:, 48:64], S.pD[:, 16:32],
                   wait=[tpj], inc="vS")
        tm1 = red_(P, "vector", S.colsc[:, 0:1], S.e2[:, 48:64], ALU.max,
                   wait=[tlg], inc="vS")
        tsc = ts_(P, "vector", S.e1[:, 32:48], S.e2[:, 48:64],
                  S.colsc[:, 0:1], op0=ALU.subtract, wait=[tm1], inc="vS")
        tex = act(P, "scalar", S.e1[:, 48:64], S.e1[:, 32:48], AF.Exp,
                  wait=[tsc], inc="sS")
        tsm = red_(P, "vector", S.colsc[:, 1:2], S.e1[:, 48:64], ALU.add,
                   wait=[tex], inc="vS")
        trc = recip_(P, S.colsc[:, 2:3], S.colsc[:, 1:2], wait=[tsm], inc="vS")
        tw = ts_(P, "vector", S.e1[:, 32:48], S.e1[:, 48:64],
                 S.colsc[:, 2:3], op0=ALU.mult, wait=[trc], inc="vS")
        tm1b = red_(P, "vector", S.colsc[:, 3:4], S.e1[:, 32:48], ALU.max,
                    wait=[tw], inc="vS")
        tim = ts_(P, "vector", S.e1[:, 48:64], S.e1[:, 32:48],
                  S.colsc[:, 3:4], op0=ALU.is_ge, wait=[tm1b], inc="vS")
        tcn = red_(P, "vector", S.colsc[:, 4:5], S.e1[:, 48:64], ALU.add,
                   wait=[tim], inc="vS")
        twm = stt_(P, "vector", S.e1[:, 64:80], S.e1[:, 48:64], -1e9,
                   S.e1[:, 32:48], ALU.mult, ALU.add, wait=[tim], inc="vS")
        tm2 = red_(P, "vector", S.colsc[:, 5:6], S.e1[:, 64:80], ALU.max,
                   wait=[twm], inc="vS")
        tcg = ts_(P, "vector", S.colsc[:, 6:7], S.colsc[:, 4:5], 2.0,
                  op0=ALU.is_ge, wait=[tcn], inc="vS")
        tdd = tt_(P, "vector", S.colsc[:, 7:8], S.colsc[:, 3:4],
                  S.colsc[:, 5:6], ALU.subtract, wait=[tm2, tm1b], inc="vS")
        tt3 = tt_(P, "vector", S.colsc[:, 8:9], S.colsc[:, 6:7],
                  S.colsc[:, 7:8], ALU.mult, wait=[tcg, tdd], inc="vS")
        tta = tt_(P, "vector", S.colsc[:, 9:10], S.colsc[:, 5:6],
                  S.colsc[:, 8:9], ALU.add, wait=[tt3], inc="vS")
        tind = ts_(P, "vector", S.e1[:, 48:64], S.e1[:, 32:48],
                   S.colsc[:, 9:10], op0=ALU.is_ge, wait=[tta], inc="vS")
        tgw = tt_(P, "vector", S.gw1[:, tt * NEXP:(tt + 1) * NEXP],
                  S.e1[:, 32:48], S.e1[:, 48:64], ALU.mult, wait=[tind], inc="vS")
        tcs = mm(P, S.pD[:, 32:48], S.c_triu[:], S.la1[:, hsl],
                 wait=[tla, tgw, tsg], inc="tS")
        tcc = cpy_(P, "vector", S.cc1[:, hsl], S.pD[:, 32:48], wait=[tcs], inc="vS")
        tga = act(P, "scalar", S.gam1[:, hsl], S.cc1[:, hsl], AF.Exp,
                  wait=[tcc], inc="sS")
        tclm = mm(P, S.pD[0:1, 48:64], S.c_onec_f[:], S.la1[:, hsl],
                  wait=[tcs], inc="tS")
        tclv = cpy_(P, "vector", S.rowsc[0:1, 0:16], S.pD[0:1, 48:64],
                    wait=[tclm], inc="vS")
        tclb = mm(P, S.pD[:, 64:80], S.c_ones_f[:], S.rowsc[0:1, 0:16],
                  wait=[tclv], inc="tS")
        tcle = cpy_(P, "vector", S.e2[:, 0:16], S.pD[:, 64:80],
                    wait=[tclb, buf_free], inc="vS")
        tgl = act(P, "scalar", S.gl1[:, hsl], S.e2[:, 0:16], AF.Exp,
                  wait=[tcle], inc="sS")
        tdm = tt_(P, "vector", S.e2[:, 16:32], S.e2[:, 0:16], S.cc1[:, hsl],
                  ALU.subtract, wait=[tcle], inc="vS")
        tdc = act(P, "scalar", S.dc1[:, hsl], S.e2[:, 16:32], AF.Exp,
                  wait=[tdm], inc="sS")
        pD_free = [tcle, tcc, tclv]
        buf_free = None
        tqb = act(P, "scalar", S.e2[:], S.pA[:, 0:1024], AF.Relu,
                  wait=[tpj, tgl, tdc], inc="sS")
        tqa = tt_(P, "vector", S.e1[:], S.pA[:, 0:1024], S.e2[:], ALU.subtract,
                  wait=[tpj, tgw, tqb], inc="vS")
        tqc = act(P, "scalar", S.sqs[:], S.e1[:], AF.Exp, wait=[tqa], inc="sS")
        tqd = tt_(P, "vector", S.q1tmp[:], S.sqs[:], S.e2[:], ALU.add,
                  wait=[tqc, tqb], inc="vS")
        pA_free = [tqa, tqb]
        ttr = None
        for c8 in range(8):
            ttr = mm(P, S.pA[:, c8 * 128:(c8 + 1) * 128],
                     S.q1tmp[:, c8 * 128:(c8 + 1) * 128], S.c_idb[:],
                     wait=[tqd, pA_free], inc="tS")
        tqf = cpy_(P, "scalar", S.q1fm[:], S.pA[:, 0:1024], wait=[ttr], inc="sS")
        pA_free = tqf
        tkb = act(P, "scalar", S.e2[:], S.pB[:, 0:1024], AF.Relu,
                  wait=[tqf, tqd], inc="sS")
        tka = tt_(P, "vector", S.e1[:], S.pB[:, 0:1024], S.e2[:], ALU.subtract,
                  wait=[tpj, tqd, tkb], inc="vS")
        tkc = act(P, "scalar", S.sqs[:], S.e1[:], AF.Exp, wait=[tka], inc="sS")
        tkd = tt_(P, "vector", S.k1tm[:], S.sqs[:], S.e2[:], ALU.add,
                  wait=[tkc, tkb], inc="vS")
        pB_free = [tka, tkb]
        ttr = None
        for c8 in range(8):
            ttr = mm(P, S.pB[:, c8 * 128:(c8 + 1) * 128],
                     S.k1tm[:, c8 * 128:(c8 + 1) * 128], S.c_idb[:],
                     wait=[tkd, pB_free], inc="tS")
        tkf = cpy_(P, "scalar", S.k1fm[:], S.pB[:, 0:1024], wait=[ttr], inc="sS")
        pB_free = tkf
        vview = S.v1aug[:].rearrange("p (h c) -> p h c", h=16)
        pcv = S.pC[:, 0:1024].rearrange("p (h c) -> p h c", h=16)
        tvv = P.emit("vector", lambda e, o=vview[:, :, 0:64], i=pcv:
                     e.tensor_copy(out=o, in_=i), wait=[tpj, tkd], inc="vS",
                     reads=_nm(pcv), writes=_nm(vview))
        tvo = mset_(P, "vector", vview[:, :, 64:65], 1.0, wait=[tvv], inc="vS")
        pC_free = [tvv]
        pE_bc_free = pE_free
        for h in range(H1):
            cq = tt * H1 + h
            hb, hp = h // 2, (h % 2) * 64
            ccol = S.cc1[:, cq:cq + 1]
            trw = mm(P, S.pE[0:1, 128:256], ccol, S.c_idf[:],
                     wait=[tcc, pE_bc_free], inc="tS")
            trv = cpy_(P, "vector", S.rowsc[:], S.pE[0:1, 128:256],
                       wait=[trw], inc="vS")
            tbc = mm(P, S.pE[:, 0:128], S.c_ones_f[:], S.rowsc[:],
                     wait=[trv], inc="tS")
            tu = act(P, "scalar", S.d1m[:], S.pE[:, 0:128], AF.Relu,
                     scale=-1.0, bias=ccol, wait=[tbc], inc="sS")
            tDq = act(P, "scalar", S.d1sb[:], S.d1m[:], AF.Exp, scale=-1.0,
                      wait=[tu], inc="sS")
            tDm = tt_(P, "vector", S.d1m[:], S.d1sb[:], S.c_triu[:], ALU.mult,
                      wait=[tDq], inc="vS")
            kfm_s = S.k1fm[hp:hp + 64, hb * 128:(hb + 1) * 128]
            qfm_s = S.q1fm[hp:hp + 64, hb * 128:(hb + 1) * 128]
            tgp = mm(P, S.pD[:, 128:256], kfm_s, qfm_s,
                     wait=[tkf, tqf, pD_free], inc="tS")
            tpd = tt_(P, "vector", S.pd1sb[:], S.pD[:, 128:256], S.d1m[:],
                      ALU.mult, wait=[tgp, tDm], inc="vS")
            pD_free = tpd
            va = S.v1aug[:, h * 65:(h + 1) * 65]
            sst = S.s1all[hp:hp + 64, hb * 65:(hb + 1) * 65]
            to1 = mm(P, S.pE[:, 256:321], S.pd1sb[:], va, start=True,
                     stop=True, wait=[tpd, tvo, st_rdy], inc="tS")
            to2 = mm(P, S.pE[:, 328:393], qfm_s, sst, start=True,
                     stop=True, wait=[tqf], inc="tS")
            tos = cpy_(P, "vector", S.o65[:], S.pE[:, 328:393], wait=[to2], inc="vS")
            tos2 = cpy_(P, "vector", S.o65b[:], S.pE[:, 256:321], wait=[to2, tos],
                        inc="vS")
            tom = ts_(P, "vector", S.o65[:], S.o65[:], S.gam1[:, cq:cq + 1],
                      op0=ALU.mult, wait=[tga], inc="vS")
            tom = tt_(P, "vector", S.o65b[:], S.o65[:], S.o65b[:], ALU.add,
                      wait=[tom], inc="vS")
            tde = ts_(P, "vector", S.colsc[:, 0:1], S.o65b[:, 64:65], 1e-6,
                      op0=ALU.add, wait=[tom], inc="vS")
            trcp = recip_(P, S.colsc[:, 1:2], S.colsc[:, 0:1], wait=[tde], inc="vS")
            tout = ts_(P, "vector", S.o1tm[:, h * 64:(h + 1) * 64],
                       S.o65b[:, 0:64], S.colsc[:, 1:2], op0=ALU.mult,
                       wait=[trcp], inc="vS")
            pE_bc_free = [tout, tos2]
            tdv = ts_(P, "vector", S.dv1[:], va, S.dc1[:, cq:cq + 1],
                      op0=ALU.mult, wait=[tvo, tdc], inc="vS")
            tgli = ts_(P, "vector", S.glib[:], S.c_id64s[:],
                       S.gl1[:, cq:cq + 1], op0=ALU.mult, wait=[tgl], inc="vS")
            ts1 = mm(P, S.pC[0:64, 0:65],
                     S.k1tm[:, h * 64:(h + 1) * 64], S.dv1[:],
                     start=True, stop=False,
                     wait=[tdv, tkd, pC_free, st_rdy], inc="tS")
            ts2 = mm(P, S.pC[0:64, 0:65], S.glib[hp:hp + 64, :], sst,
                     start=False, stop=True, wait=[tgli, to2], inc="tS")
            tse = cpy_(P, "vector", sst, S.pC[0:64, 0:65], wait=[ts2], inc="vS")
            st_rdy = tse
            pC_free = tse
        pE_free = pE_bc_free
        ttr = None
        for c8 in range(8):
            ttr = mm(P, S.pA[:, c8 * 128:(c8 + 1) * 128],
                     S.o1tm[:, c8 * 128:(c8 + 1) * 128], S.c_idb[:],
                     wait=[tout, pA_free], inc="tS")
        tof = cpy_(P, "scalar", S.o1fm[:], S.pA[:, 0:1024], wait=[ttr], inc="sS")
        pA_free = tof
        two = None
        for pc8 in range(8):
            lhs = S.o1fm[:, pc8 * 128:(pc8 + 1) * 128]
            mm(P, S.pB[:, 0:512], lhs, S.wo1mov[:, pc8 * 1024:pc8 * 1024 + 512],
               start=(pc8 == 0), stop=(pc8 == 7),
               wait=[tof, P.rdy["wo1mov"], pB_free], inc="tS")
            two = mm(P, S.pB[:, 512:1024], lhs,
                     S.wo1mov[:, pc8 * 1024 + 512:(pc8 + 1) * 1024],
                     start=(pc8 == 0), stop=(pc8 == 7), inc="tS")
        td2 = P.dma("sync", S.xt[p][:], xtile(S, tt, "xcur"), wait=[xuse[p], t1])
        txo = tt_(P, "vector", S.xo[p][:], S.pB[:, 0:1024], S.xt[p][:],
                  ALU.add, wait=[two, td2, xo_use[p]], inc="vS")
        tst = P.dma("sync", xtile(S, tt, "xcur"), S.xo[p][:], wait=[txo])
        xuse[p] = [txo]
        xo_use[p] = [tst]
        pB_free = txo
    P.tap("xl1", S.xo[(NT - 1) % 2][:], (128, D), F32, wait=[txo])
    if "xcur1" in DEBUG_TAPS:
        tx1 = P.nc.dram_tensor("tap_xcur1", [B, T, D], F32, kind="ExternalOutput").ap()
        P.dma("sync", tx1, S.xcur, wait=[[tst], txo])
    P.tap("o1tm", S.o1tm[:], (128, 1024), BF16, wait=[tout])
    P.rdy["l1attn"] = [[tst], txo, st_rdy, two, tgw]
    P.pf = {"pA": pA_free, "pB": pB_free, "pC": pC_free, "pD": pD_free,
            "pE": pE_free}


def moe(P, S):
    bar = P.barrier()
    acc_sl = lambda tt: S.accH[:, (tt % 16) * 1024:((tt % 16) + 1) * 1024]
    we_use = [bar, bar]
    eb_use = bar
    acc_use = {}
    out_last = None
    pp = [P.pf["pA"], P.pf["pB"]]
    psums = [S.pA, S.pB]
    for half in range(2):
        tiles = list(range(half * 16, (half + 1) * 16))
        tin = {}
        for tt in tiles:
            tin[tt] = P.dma("sync", acc_sl(tt), xtile(S, tt, "xcur"),
                            wait=[acc_use.get(tt % 16), bar])
        tacc = {tt: tin[tt] for tt in tiles}
        for e in range(NEXP):
            buf = e % 2
            twl = None
            for kc in range(8):
                twl = P.dma("gpsimd", S.wemov[buf][:, kc * 1024:(kc + 1) * 1024],
                            S.expTW[e, kc * 128:(kc + 1) * 128, :],
                            wait=[we_use[buf]])
            tbe = P.dma("gpsimd", S.ebrow[:], S.expB[e:e + 1, :], wait=[eb_use])
            for i, tt in enumerate(tiles):
                ps = psums[i % 2]
                tb1 = mm(P, ps[:, 0:512], S.c_ones_bf[:], S.ebrow[0:1, 0:512],
                         start=True, stop=False, wait=[tbe, twl, pp[i % 2]],
                         inc="tS")
                mm(P, ps[:, 512:1024], S.c_ones_bf[:], S.ebrow[0:1, 512:1024],
                   start=True, stop=False, inc="tS")
                tme = None
                for kc in range(8):
                    lhs = S.actT[:, tt * D + kc * 128:tt * D + (kc + 1) * 128]
                    mm(P, ps[:, 0:512], lhs,
                       S.wemov[buf][:, kc * 1024:kc * 1024 + 512],
                       start=False, stop=(kc == 7), inc="tS")
                    tme = mm(P, ps[:, 512:1024], lhs,
                             S.wemov[buf][:, kc * 1024 + 512:(kc + 1) * 1024],
                             start=False, stop=(kc == 7), inc="tS")
                tre = act(P, "scalar", S.sqs[:], ps[:, 0:1024], AF.Relu,
                          scale=S.gw1[:, tt * NEXP + e:tt * NEXP + e + 1],
                          wait=[tme], inc="sS")
                tad = tt_(P, "vector", acc_sl(tt), acc_sl(tt), S.sqs[:], ALU.add,
                          wait=[tre, tacc[tt]], inc="vS")
                tacc[tt] = tad
                pp[i % 2] = [tre, tad]
            we_use[buf] = tme
            eb_use = tb1
        for tt in tiles:
            tso = P.dma("sync", xtile(S, tt, "out"), acc_sl(tt), wait=[tacc[tt]])
            acc_use[tt % 16] = tso
            out_last = tso
    P.rdy["out_done"] = out_last
    P.pf["pA"] = pp[0]
    P.pf["pB"] = pp[1]


def kernel(**inputs):
    g = lambda n: np.asarray(inputs[n], dtype=np.float32)
    x_full = g("L0_attn_x")

    W0cat = np.ascontiguousarray(np.concatenate(
        [g("L0_attn_Wq").T, g("L0_attn_Wk").T, g("L0_attn_Wv").T,
         g("L0_attn_Wg").T, g("L0_attn_Wa").T, g("L0_attn_Wb").T], axis=1))
    b0cat = np.ascontiguousarray(np.concatenate(
        [g("L0_attn_bq"), g("L0_attn_bk"), g("L0_attn_bv"), g("L0_attn_bg")])
        .reshape(1, 512))
    bab0 = np.ascontiguousarray(
        np.concatenate([g("L0_attn_ba"), g("L0_attn_bb")]).reshape(1, 2))
    Wo0T = np.ascontiguousarray(g("L0_attn_Wo").T)
    bo0 = np.ascontiguousarray(g("L0_attn_bo").reshape(1, D))
    WgT = np.ascontiguousarray(g("L0_ffn_w_gate").T)
    WuT = np.ascontiguousarray(g("L0_ffn_w_up").T)
    WdT = np.ascontiguousarray(g("L0_ffn_w_down").T)
    W1cat = np.ascontiguousarray(np.concatenate(
        [g("L1_attn_Wq"), g("L1_attn_Wk"), g("L1_attn_Wv"),
         g("L1_attn_Wa"), g("L1_ffn_gate_w").T], axis=1))
    ba1 = np.ascontiguousarray(g("L1_attn_ba").reshape(1, H1))
    Wo1 = np.ascontiguousarray(g("L1_attn_Wo"))
    expTW = np.ascontiguousarray(g("L1_ffn_expert_w").transpose(0, 2, 1))
    expB = np.ascontiguousarray(g("L1_ffn_expert_b"))

    nc = build()
    in_maps = []
    for c in range(NCORES):
        shard = np.ascontiguousarray(x_full[2 * c:2 * c + 2])
        in_maps.append({
            "x": shard, "W0cat": W0cat, "b0cat": b0cat, "bab0": bab0,
            "Wo0T": Wo0T, "bo0": bo0, "WgT": WgT, "WuT": WuT, "WdT": WdT,
            "W1cat": W1cat, "ba1": ba1, "Wo1": Wo1, "expTW": expTW,
            "expB": expB,
        })
    res = run_bass_kernel_spmd(nc, in_maps, core_ids=list(range(NCORES)))
    out = np.concatenate([np.asarray(r["out"]) for r in res.results], axis=0)
    return out.astype(np.float32)
